# version 04
# baseline (speedup 1.0000x reference)
import math
import numpy as np
import ml_dtypes
import contextlib

import concourse.bass as bass
import concourse.mybir as mybir
from concourse.bass_utils import run_bass_kernel_spmd

F32 = mybir.dt.float32
BF16 = mybir.dt.bfloat16
F16 = mybir.dt.float16
U16 = mybir.dt.uint16
I32 = mybir.dt.int32
AF = mybir.ActivationFunctionType
OP = mybir.AluOpType
AX = mybir.AxisListType

BF = ml_dtypes.bfloat16

B, T, D = 4, 4096, 2048
TPC = 2048
NT = TPC // 128
ND = D // 128
I_GU = 5632
H1, HKV1, HD1 = 16, 4, 128
E_MOE, I_MOE, TOPK = 16, 1408, 6
IS_MOE = 2816

ENGINES = ("sync", "gpsimd", "scalar", "vector", "tensor")


class Em:

    def __init__(self, nc, ctx):
        self.nc = nc
        self.ctx = ctx
        self.q = {e: [] for e in ENGINES}
        self.cnt = {}
        self._pools = {}
        self._live = []
        self._nsem = 0
        self._bar = None
        self._barcnt = 0

    def sem(self, name, kind="c"):
        self._nsem += 1
        h = self.ctx.enter_context(self.nc.semaphore("s%d" % self._nsem))
        self._live.append((kind, h))
        self.cnt[h] = 0
        return h

    def new_phase(self):
        if self._bar is None:
            self._bar = self.ctx.enter_context(self.nc.semaphore("bar"))
        for kind, h in self._live:
            v = self.cnt[h]
            if v > 0:
                for e in ENGINES:
                    self.on(e, lambda E, h=h, v=v: E.wait_ge(h, v))
        for e in ENGINES:
            self.on(e, lambda E, s=self._bar: E.sem_inc(s, 1))
        self._barcnt += len(ENGINES)
        tgt = self._barcnt
        self.on("gpsimd", lambda E, s=self._bar, v=tgt: E.wait_ge(s, v))
        for e in ENGINES:
            self.on(e, lambda E, s=self._bar, v=tgt: E.wait_ge(s, v))
        for e in ENGINES:
            self.on(e, lambda E, s=self._bar: E.sem_inc(s, 1))
        self._barcnt += len(ENGINES)
        tgt2 = self._barcnt
        for e in ENGINES:
            self.on(e, lambda E, s=self._bar, v=tgt2: E.wait_ge(s, v))
        self._live = []

    def on(self, eng, fn):
        self.q[eng].append(fn)

    def bump(self, sem, by):
        self.cnt[sem] += by
        return self.cnt[sem]

    def val(self, sem):
        return self.cnt[sem]

    def dma(self, eng, out, in_, sem, **kw):
        v = self.bump(sem, 16)
        self.on(eng, lambda E, out=out, in_=in_, kw=kw, sem=sem:
                E.dma_start(out=out, in_=in_, **kw).then_inc(sem, 16))
        return v

    def dma_tp(self, eng, out, in_, sem):
        v = self.bump(sem, 16)
        self.on(eng, lambda E, out=out, in_=in_, sem=sem:
                E.dma_start_transpose(out=out, in_=in_).then_inc(sem, 16))
        return v

    def wait(self, eng, sem, v=None):
        if v is None:
            v = self.val(sem)
        if v <= 0:
            return
        self.on(eng, lambda E, sem=sem, v=v: E.wait_ge(sem, v))

    def op(self, eng, meth, sem=None, **kw):
        if sem is not None:
            v = self.bump(sem, 1)

            def fn(E, meth=meth, kw=kw, sem=sem):
                getattr(E, meth)(**kw).then_inc(sem, 1)
            self.on(eng, fn)
            return v
        else:
            def fn(E, meth=meth, kw=kw):
                getattr(E, meth)(**kw)
            self.on(eng, fn)
            return None

    def mm(self, out, lhsT, rhs, start, stop, sem=None, force=False):
        if (stop or force) and sem is not None:
            v = self.bump(sem, 1)

            def fn(E, out=out, lhsT=lhsT, rhs=rhs, start=start, stop=stop, sem=sem):
                E.matmul(out=out, lhsT=lhsT, rhs=rhs, start=start, stop=stop,
                         skip_group_check=True).then_inc(sem, 1)
            self.on("tensor", fn)
            return v

        def fn(E, out=out, lhsT=lhsT, rhs=rhs, start=start, stop=stop):
            E.matmul(out=out, lhsT=lhsT, rhs=rhs, start=start, stop=stop,
                     skip_group_check=True)
        self.on("tensor", fn)
        return None

    def finish(self, block):
        for e, sec in (("sync", block.sync), ("gpsimd", block.gpsimd),
                       ("scalar", block.scalar), ("vector", block.vector),
                       ("tensor", block.tensor)):
            ops = self.q[e]

            def body(E, ops=ops):
                for fn in ops:
                    fn(E)
            sec(body)


def emit_ln_norm(em, cst, r_chunks, rsum, sqsum, out_chunks, s_v, s_a, n=D):
    mean, var, t3, rstd = cst["m1"], cst["m2"], cst["m3"], cst["m4"]
    em.wait("vector", s_v)
    v = em.op("vector", "tensor_single_scalar", sem=s_v, out=mean, in_=rsum,
              scalar=1.0 / n, op=OP.mult)
    em.wait("vector", s_v, v)
    v = em.op("vector", "tensor_tensor", sem=s_v, out=var, in0=mean, in1=mean,
              op=OP.mult)
    em.wait("vector", s_v, v)
    v = em.op("vector", "tensor_single_scalar", sem=s_v, out=t3, in_=sqsum,
              scalar=1.0 / n, op=OP.mult)
    em.wait("vector", s_v, v)
    v = em.op("vector", "tensor_tensor", sem=s_v, out=var, in0=t3, in1=var,
              op=OP.subtract)
    em.wait("scalar", s_v, v)
    v = em.op("scalar", "activation", sem=s_a, out=t3, in_=var, func=AF.Ln,
              bias=cst["ceps"])
    em.wait("scalar", s_a, v)
    v = em.op("scalar", "activation", sem=s_a, out=rstd, in_=t3,
              func=AF.Exp, scale=cst["cmhalf"])
    em.wait("vector", s_a, v)
    for rc, oc in zip(r_chunks, out_chunks):
        em.op("vector", "tensor_scalar", sem=s_v, out=oc, in0=rc,
              scalar1=mean, scalar2=rstd, op0=OP.subtract, op1=OP.mult)
    return em.val(s_v)


def emit_l0a(em, nc, sb, io, psA, psB, stop_after=None):
    s_ld = em.sem("a_ld", "s")
    s_gld = em.sem("a_gld", "g")
    s_gp = [em.sem("a_gp0", "g"), em.sem("a_gp1", "g")]
    s_sp = [em.sem("a_sp0", "s"), em.sem("a_sp1", "s")]
    s_wsl = [em.sem("a_wsl0", "s"), em.sem("a_wsl1", "s"), em.sem("a_wsl2", "s")]
    s_x = em.sem("a_x", "s")
    s_gx = em.sem("a_gx", "g")
    s_mm = em.sem("a_mm")
    s_v = em.sem("a_v")
    s_a = em.sem("a_a")
    s_st = em.sem("a_st", "s")

    cst = sb["cst"]
    xT = sb["arenaA"]
    xp = sb["arenaB"]
    ysT = sb["arenaA"]

    def tile(t_, k):
        return t_[:, 2048 * k:2048 * (k + 1)]

    stage_store = [0, 0]
    xcast_done = {}
    for i, (nsrc, ndst) in enumerate((("x0_pre", "xbf_pre"), ("x0_own", "xbf_own"))):
        for t in range(NT):
            tg = (i * NT + t) % 2
            stg = sb["xstage"][tg]
            em.wait("gpsimd", s_sp[tg], stage_store[tg])
            em.wait("gpsimd", s_gp[tg], em.val(s_gp[tg]))
            em.dma("gpsimd", stg, io[nsrc][128 * t:128 * (t + 1), :], s_gp[tg])
            em.wait("sync", s_gp[tg], em.val(s_gp[tg]))
            em.dma("sync", io[ndst][128 * t:128 * (t + 1), :], stg, s_sp[tg])
            stage_store[tg] = em.val(s_sp[tg])
        xcast_done[ndst] = (em.val(s_sp[0]), em.val(s_sp[1]))

    wstage_store = [0, 0]
    for k in range(ND):
        stg = sb["wstage"][k % 2]
        em.wait("gpsimd", s_sp[k % 2], wstage_store[k % 2])
        em.wait("gpsimd", s_gp[k % 2], em.val(s_gp[k % 2]))
        em.dma("gpsimd", stg, io["woutT"][128 * k:128 * (k + 1), :], s_gp[k % 2])
        em.wait("sync", s_gp[k % 2], em.val(s_gp[k % 2]))
        em.dma("sync", io["woutT_s"][k, :, :], stg, s_sp[k % 2])
        wstage_store[k % 2] = em.val(s_sp[k % 2])
    wouts_done = (em.val(s_sp[0]), em.val(s_sp[1]))

    em.wait("gpsimd", s_gld, em.val(s_gld))
    em.dma("gpsimd", sb["wdt"], io["wdtT_r"], s_gld)
    em.dma("sync", sb["acraw"], io["logA_t"], s_ld)
    em.dma("sync", sb["bdt"], io["b_dt"], s_ld)
    em.dma("sync", sb["selmat"], io["selmat"], s_ld)
    small_ld = em.val(s_ld)
    small_gld = em.val(s_gld)
    em.wait("scalar", s_ld, small_ld)
    em.wait("scalar", s_gld, small_gld)
    em.op("scalar", "activation", sem=s_a, out=sb["acont"], in_=sb["acraw"],
          func=AF.Exp)
    em.wait("vector", s_a)
    acont_ready = em.op("vector", "tensor_single_scalar", sem=s_v, out=sb["acont"],
                        in_=sb["acont"], scalar=-1.0, op=OP.mult)

    def ssm_half(which):
        src = io["xbf_pre"] if which == "pre" else io["xbf_own"]
        em.wait("sync", s_x)
        em.wait("sync", s_ld)
        em.wait("sync", s_st)
        em.wait("sync", s_gld)
        em.wait("sync", s_sp[0])
        em.wait("sync", s_sp[1])
        for k in range(ND):
            em.dma_tp("sync", tile(xT, k), src[:, 128 * k:128 * (k + 1)], s_x)
        tp_done = em.val(s_x)
        em.wait("gpsimd", s_x, tp_done)

        for e in range(ND):
            wb = sb["wincol"][e % 2]
            em.wait("gpsimd", s_mm, em.val(s_mm))
            em.wait("gpsimd", s_gld, em.val(s_gld))
            em.dma("gpsimd", wb, io["winT_t"][e], s_gld)
            em.wait("tensor", s_gld, em.val(s_gld))
            if e == 0:
                em.wait("tensor", s_x, tp_done)
            ps = psA if e % 2 == 0 else psB
            em.wait("tensor", s_v)
            for k in range(ND):
                for n in range(4):
                    em.mm(ps[:, 512 * n:512 * (n + 1)],
                          wb[:, 128 * k:128 * (k + 1)],
                          tile(xT, k)[:, 512 * n:512 * (n + 1)],
                          start=(k == 0), stop=(k == ND - 1),
                          sem=s_mm if n == 3 else None)
            em.wait("vector", s_mm)
            em.op("vector", "tensor_copy", sem=s_v, out=tile(xp, e), in_=ps[:, :])
        xp_done = em.val(s_v)

        em.wait("tensor", s_ld, small_ld)
        em.wait("tensor", s_gld, small_gld)
        em.wait("tensor", s_v, xp_done)
        dtps = psA[0:32, :]
        for k in range(ND):
            for n in range(4):
                em.mm(dtps[:, 512 * n:512 * (n + 1)],
                      sb["wdt"][:, 32 * k:32 * (k + 1)],
                      tile(xp, k)[:, 512 * n:512 * (n + 1)],
                      start=(k == 0), stop=(k == ND - 1),
                      sem=s_mm if n == 3 else None)
        em.wait("scalar", s_mm)
        v = em.op("scalar", "activation", sem=s_a, out=sb["dt1"], in_=dtps,
                  func=AF.Abs, bias=sb["bdt"])
        em.wait("scalar", s_a, v)
        v = em.op("scalar", "activation", sem=s_a, out=sb["dt2"], in_=sb["dt1"],
                  func=AF.Exp, scale=cst["cm1"][0:32, 0:1])
        em.wait("scalar", s_a, v)
        v = em.op("scalar", "activation", sem=s_a, out=sb["dt1"], in_=sb["dt2"],
                  func=AF.Ln, bias=cst["c1"][0:32, 0:1])
        em.wait("scalar", s_a, v)
        v = em.op("scalar", "activation", sem=s_a, out=sb["dt2"], in_=dtps,
                  func=AF.Relu, bias=sb["bdt"])
        em.wait("vector", s_a, v)
        em.wait("vector", s_v)
        dt_ready = em.op("vector", "tensor_tensor", sem=s_v, out=sb["dtbf"],
                         in0=sb["dt1"], in1=sb["dt2"], op=OP.add)

        em.wait("tensor", s_v, dt_ready)
        agen_prev = 0
        scan_prev = 0
        for k in range(ND):
            ab = sb["a2"][k % 2]
            em.wait("tensor", s_a, agen_prev)
            for n in range(4):
                em.mm(psB[:, 512 * n:512 * (n + 1)],
                      sb["selmat"][:, 128 * k:128 * (k + 1)],
                      sb["dtbf"][:, 512 * n:512 * (n + 1)], start=True, stop=True,
                      sem=s_mm)
            em.wait("scalar", s_mm)
            em.wait("scalar", s_v, max(acont_ready, scan_prev))
            for n in range(4):
                em.op("scalar", "activation", sem=s_a,
                      out=ab[:, 512 * n:512 * (n + 1)],
                      in_=psB[:, 512 * n:512 * (n + 1)], func=AF.Exp,
                      scale=sb["acont"][:, k:k + 1])
            agen_prev = em.val(s_a)
            em.wait("vector", s_a, agen_prev)
            em.wait("vector", s_v)
            if which == "pre":
                for cc in range(4):
                    init = 0.0 if cc == 0 else sb["states"][:, k:k + 1]
                    v = em.op("vector", "tensor_tensor_scan", sem=s_v,
                              out=sb["prescan"], data0=ab[:, 512 * cc:512 * (cc + 1)],
                              data1=tile(xp, k)[:, 512 * cc:512 * (cc + 1)],
                              initial=init, op0=OP.mult, op1=OP.add)
                    em.wait("vector", s_v, v)
                    v = em.op("vector", "tensor_copy", sem=s_v,
                              out=sb["states"][:, k:k + 1],
                              in_=sb["prescan"][:, 511:512])
                    em.wait("vector", s_v, v)
                scan_prev = em.val(s_v)
            else:
                scan_prev = em.op("vector", "tensor_tensor_scan", sem=s_v,
                                  out=tile(ysT, k), data0=ab, data1=tile(xp, k),
                                  initial=sb["states"][:, k:k + 1],
                                  op0=OP.mult, op1=OP.add)
        return em.val(s_v)

    def dump_and_stop(aps):
        em.wait("sync", s_v)
        em.wait("sync", s_a)
        em.wait("sync", s_mm)
        for src, dst in aps:
            em.dma("sync", dst, src, s_st)
        em.wait("sync", s_st)

    if stop_after != "ownonly":
        ssm_half("pre")
    if stop_after == "pre":
        dump_and_stop([(xp[:, :], io["dbg_big"][:, 0:32768]),
                       (sb["states"], io["dbg_small"][0:128, 0:16]),
                       (sb["dtbf"], io["dbg_small"][0:32, 16:1040].bitcast(BF16))])
        return
    em.wait("sync", s_v)
    em.wait("sync", s_mm)
    scan_done = ssm_half("own")
    if stop_after in ("own", "ownonly"):
        dump_and_stop([(xp[:, :], io["dbg_big"][:, 0:32768]),
                       (ysT[:, 0:2048].bitcast(F32), io["dbg_big2"][:, 0:1024].bitcast(F32)),
                       (sb["states"], io["dbg_small"][0:128, 0:16]),
                       (sb["dtbf"], io["dbg_small"][0:32, 16:1040].bitcast(BF16)),
                       (sb["a2"][1], io["dbg_big2"][:, 2048:4096])])
        return

    em.wait("tensor", s_v, scan_done)
    rhalf = sb["rhalf"]
    store_prev = 0
    evac_done = 0
    for th in range(2):
        for nch in range(4):
            em.wait("tensor", s_v, evac_done)
            mm_after_k = {}
            for k in range(ND):
                wb = sb["wslice"][k % 3]
                if k >= 3 or nch > 0 or th > 0:
                    em.wait("sync", s_mm, mm_after_k[k - 3] if k >= 3 else mm_prev_pass)
                em.wait("sync", s_sp[0], wouts_done[0])
                em.wait("sync", s_sp[1], wouts_done[1])
                em.dma("sync", wb, io["woutT_s"][k, :, 512 * nch:512 * (nch + 1)],
                       s_wsl[k % 3])
                em.wait("tensor", s_wsl[k % 3], em.val(s_wsl[k % 3]))
                for t in range(8):
                    tt = 8 * th + t
                    ps = (psA if t < 4 else psB)[:, 512 * (t % 4):512 * (t % 4 + 1)]
                    em.mm(ps, tile(ysT, k)[:, 128 * tt:128 * (tt + 1)], wb,
                          start=(k == 0), stop=(k == ND - 1),
                          sem=s_mm if t == 7 else None, force=(t == 7))
                mm_after_k[k] = em.val(s_mm)
            mm_prev_pass = em.val(s_mm)
            em.wait("vector", s_mm)
            for t in range(8):
                tt = 8 * th + t
                ps = (psA if t < 4 else psB)[:, 512 * (t % 4):512 * (t % 4 + 1)]
                xc = sb["x0t"][t % 2]
                em.wait("sync", s_v)
                em.dma("sync", xc,
                       io["xbf_own"][128 * tt:128 * (tt + 1), 512 * nch:512 * (nch + 1)],
                       s_x)
                em.wait("vector", s_x)
                rc = rhalf[:, (t * 4 + nch) * 512:(t * 4 + nch + 1) * 512]
                em.op("vector", "scalar_tensor_tensor", sem=s_v, out=rc,
                      in0=xc, scalar=1.0, in1=ps, op0=OP.mult, op1=OP.add,
                      accum_out=sb["psums"][:, (t * 4 + nch):(t * 4 + nch) + 1])
            evac_done = em.val(s_v)
        if stop_after == "ep0" and th == 0:
            em.wait("sync", s_v)
            em.dma("sync", io["dbg_big"], rhalf.bitcast(BF16)[:, 0:32768], s_st)
            em.dma("sync", io["dbg_small"][0:128, 0:32], sb["psums"], s_st)
            em.wait("sync", s_st)
            return
        for t in range(8):
            tt = 8 * th + t
            rcs = [rhalf[:, (t * 4 + n) * 512:(t * 4 + n + 1) * 512] for n in range(4)]
            em.wait("vector", s_v)
            em.wait("vector", s_st, store_prev)
            em.op("vector", "tensor_tensor", sem=s_v, out=cst["m5"],
                  in0=sb["psums"][:, t * 4:t * 4 + 1],
                  in1=sb["psums"][:, t * 4 + 1:t * 4 + 2], op=OP.add)
            em.op("vector", "tensor_tensor", sem=s_v, out=cst["m6"],
                  in0=sb["psums"][:, t * 4 + 2:t * 4 + 3],
                  in1=sb["psums"][:, t * 4 + 3:t * 4 + 4], op=OP.add)
            em.wait("vector", s_v)
            em.op("vector", "tensor_tensor", sem=s_v, out=cst["m5"],
                  in0=cst["m5"], in1=cst["m6"], op=OP.add)
            rsum_v = em.val(s_v)
            em.wait("scalar", s_v, rsum_v)
            for n in range(4):
                em.wait("scalar", s_a, em.val(s_a))
                em.op("scalar", "activation", sem=s_a, out=sb["sqscr"],
                      in_=rcs[n], func=AF.Square,
                      accum_out=sb["psq"][:, n:n + 1])
            em.wait("vector", s_a)
            em.op("vector", "tensor_tensor", sem=s_v, out=cst["m6"],
                  in0=sb["psq"][:, 0:1], in1=sb["psq"][:, 1:2], op=OP.add)
            em.op("vector", "tensor_tensor", sem=s_v, out=cst["m7"],
                  in0=sb["psq"][:, 2:3], in1=sb["psq"][:, 3:4], op=OP.add)
            em.wait("vector", s_v)
            em.op("vector", "tensor_tensor", sem=s_v, out=cst["m6"],
                  in0=cst["m6"], in1=cst["m7"], op=OP.add)
            outcs = [sb["oxt"][:, n * 512:(n + 1) * 512] for n in range(4)]
            ov = emit_ln_norm(em, cst, rcs, cst["m5"], cst["m6"], outcs, s_v, s_a)
            em.wait("sync", s_v, ov)
            store_prev = em.dma("sync", io["x1s"][128 * tt:128 * (tt + 1), :],
                                sb["oxt"], s_st)
    em.wait("sync", s_st)
    em.wait("gpsimd", s_st)
    em.wait("vector", s_st)


def emit_l0f(em, nc, sb, io, psA, psB, stop_after=None):
    s_tp = em.sem("f_tp", "s")
    s_gb = [em.sem("f_gb0", "s"), em.sem("f_gb1", "s")]
    s_bd = [em.sem("f_bd0", "s"), em.sem("f_bd1", "s"), em.sem("f_bd2", "s")]
    s_xc = [em.sem("f_xc0", "s"), em.sem("f_xc1", "s")]
    s_mm = em.sem("f_mm")
    s_v = em.sem("f_v")
    s_a = em.sem("f_a")
    s_st = em.sem("f_st", "s")

    cst = sb["cst"]
    x1T = sb["arenaA"]
    h = sb["hbuf"]

    def xtile(k):
        return x1T[:, 2048 * k:2048 * (k + 1)]

    def htile(it):
        return h[:, 1024 * it:1024 * (it + 1)]

    for k in range(ND):
        em.dma_tp("sync", xtile(k), io["x1s"][:, 128 * k:128 * (k + 1)], s_tp)
    tp_done = em.val(s_tp)
    em.wait("tensor", s_tp, tp_done)

    NI = 44
    store_prev = 0
    for th in range(2):
        t0 = th * 1024
        gb_mm = [0, 0]
        tt_done = [0, 0]
        silu_done = 0
        for it in range(NI):
            gb = sb["gbuf"][it % 2]
            em.wait("sync", s_mm, gb_mm[it % 2])
            em.wait("sync", s_gb[it % 2], em.val(s_gb[it % 2]))
            em.dma("sync", gb[:, 0:2048], io["wgu_t"][it], s_gb[it % 2])
            em.dma("sync", gb[:, 2048:4096], io["wgu_t"][it + NI], s_gb[it % 2])
            em.wait("tensor", s_gb[it % 2], em.val(s_gb[it % 2]))
            psg = psA if it % 2 == 0 else psB
            psu_off = 1024
            em.wait("tensor", s_v, tt_done[it % 2])
            for k in range(ND):
                for n in range(2):
                    em.mm(psg[:, 512 * n:512 * (n + 1)],
                          gb[:, 128 * k:128 * (k + 1)],
                          xtile(k)[:, t0 + 512 * n:t0 + 512 * (n + 1)],
                          start=(k == 0), stop=(k == ND - 1),
                          sem=s_mm if (k == ND - 1 and n == 1) else None,
                          force=(k == ND - 1 and n == 1))
                for n in range(2):
                    em.mm(psg[:, psu_off + 512 * n:psu_off + 512 * (n + 1)],
                          gb[:, 2048 + 128 * k:2048 + 128 * (k + 1)],
                          xtile(k)[:, t0 + 512 * n:t0 + 512 * (n + 1)],
                          start=(k == 0), stop=(k == ND - 1),
                          sem=s_mm if (k == ND - 1 and n == 1) else None,
                          force=(k == ND - 1 and n == 1))
            gb_mm[it % 2] = em.val(s_mm)
            em.wait("scalar", s_mm, em.val(s_mm))
            em.wait("scalar", s_v, em.val(s_v))
            sv = em.op("scalar", "activation", sem=s_a, out=sb["silu_sb"],
                       in_=psg[:, 0:1024], func=AF.Silu)
            silu_done = sv
            em.wait("vector", s_a, sv)
            em.op("vector", "tensor_tensor", sem=s_v, out=htile(it),
                  in0=sb["silu_sb"], in1=psg[:, 1024:2048], op=OP.mult)
            tt_done[it % 2] = em.val(s_v)
        h_done = em.val(s_v)

        em.wait("tensor", s_v, h_done)
        bd_mm = [0, 0, 0]
        for tp in range(4):
            for k in range(NI):
                bd = sb["bdbuf"][k % 3]
                em.wait("sync", s_mm, bd_mm[k % 3])
                em.wait("sync", s_bd[k % 3], em.val(s_bd[k % 3]))
                em.dma("sync", bd, io["wdn"][128 * k:128 * (k + 1), :], s_bd[k % 3])
                em.wait("tensor", s_bd[k % 3], em.val(s_bd[k % 3]))
                for t in range(2):
                    ps = psA if t == 0 else psB
                    for nch in range(4):
                        em.mm(ps[:, 512 * nch:512 * (nch + 1)],
                              htile(k)[:, 256 * tp + 128 * t:256 * tp + 128 * (t + 1)],
                              bd[:, 512 * nch:512 * (nch + 1)],
                              start=(k == 0), stop=(k == NI - 1),
                              sem=s_mm if (t == 1 and nch == 3) else None,
                              force=(t == 1 and nch == 3))
                bd_mm[k % 3] = em.val(s_mm)
            dn_mm = em.val(s_mm)
            for t in range(2):
                tt = 8 * th + 2 * tp + t
                ps = psA if t == 0 else psB
                em.wait("scalar", s_mm, dn_mm)
                for n in range(4):
                    em.wait("scalar", s_a, em.val(s_a))
                    em.op("scalar", "activation", sem=s_a, out=sb["sqscr"],
                          in_=ps[:, 512 * n:512 * (n + 1)], func=AF.Square,
                          accum_out=sb["psq"][:, n:n + 1])
                em.wait("vector", s_a, em.val(s_a))
                em.wait("vector", s_st, store_prev)
                em.op("vector", "tensor_tensor", sem=s_v, out=cst["m5"],
                      in0=sb["psq"][:, 0:1], in1=sb["psq"][:, 1:2], op=OP.add)
                em.op("vector", "tensor_tensor", sem=s_v, out=cst["m6"],
                      in0=sb["psq"][:, 2:3], in1=sb["psq"][:, 3:4], op=OP.add)
                em.wait("vector", s_v)
                v = em.op("vector", "tensor_tensor", sem=s_v, out=cst["m5"],
                          in0=cst["m5"], in1=cst["m6"], op=OP.add)
                em.wait("vector", s_v, v)
                v = em.op("vector", "tensor_scalar", sem=s_v, out=cst["m6"],
                          in0=cst["m5"], scalar1=1.0 / D, scalar2=1e-6,
                          op0=OP.mult, op1=OP.add)
                em.wait("scalar", s_v, v)
                v = em.op("scalar", "activation", sem=s_a, out=cst["m7"],
                          in_=cst["m6"], func=AF.Ln)
                em.wait("scalar", s_a, v)
                v = em.op("scalar", "activation", sem=s_a, out=cst["m7"],
                          in_=cst["m7"], func=AF.Exp, scale=cst["cmhalf"])
                em.wait("vector", s_a, v)
                for n in range(4):
                    xc = sb["x1c"][n % 2]
                    em.wait("sync", s_v, em.val(s_v))
                    em.dma("sync", xc,
                           io["x1s"][128 * tt:128 * (tt + 1), 512 * n:512 * (n + 1)],
                           s_xc[n % 2])
                    em.wait("vector", s_xc[n % 2], em.val(s_xc[n % 2]))
                    rc = sb["rtile"][:, 2048 * t + 512 * n:2048 * t + 512 * (n + 1)]
                    em.op("vector", "scalar_tensor_tensor", sem=s_v, out=rc,
                          in0=ps[:, 512 * n:512 * (n + 1)], scalar=cst["m7"],
                          in1=xc, op0=OP.mult, op1=OP.add,
                          accum_out=sb["psums"][:, n:n + 1])
                em.wait("vector", s_v)
                em.op("vector", "tensor_tensor", sem=s_v, out=cst["m5"],
                      in0=sb["psums"][:, 0:1], in1=sb["psums"][:, 1:2], op=OP.add)
                em.op("vector", "tensor_tensor", sem=s_v, out=cst["m6"],
                      in0=sb["psums"][:, 2:3], in1=sb["psums"][:, 3:4], op=OP.add)
                em.wait("vector", s_v)
                v = em.op("vector", "tensor_tensor", sem=s_v, out=cst["m5"],
                          in0=cst["m5"], in1=cst["m6"], op=OP.add)
                em.wait("vector", s_v, v)
                rsum_v = em.val(s_v)
                em.wait("scalar", s_v, rsum_v)
                rcs = [sb["rtile"][:, 2048 * t + 512 * n:2048 * t + 512 * (n + 1)]
                       for n in range(4)]
                for n in range(4):
                    em.wait("scalar", s_a, em.val(s_a))
                    em.op("scalar", "activation", sem=s_a, out=sb["sqscr"],
                          in_=rcs[n], func=AF.Square, accum_out=sb["psq"][:, n:n + 1])
                em.wait("vector", s_a)
                em.op("vector", "tensor_tensor", sem=s_v, out=cst["m6"],
                      in0=sb["psq"][:, 0:1], in1=sb["psq"][:, 1:2], op=OP.add)
                em.op("vector", "tensor_tensor", sem=s_v, out=cst["m2"],
                      in0=sb["psq"][:, 2:3], in1=sb["psq"][:, 3:4], op=OP.add)
                em.wait("vector", s_v)
                v = em.op("vector", "tensor_tensor", sem=s_v, out=cst["m6"],
                          in0=cst["m6"], in1=cst["m2"], op=OP.add)
                em.wait("vector", s_v, v)
                outcs = [sb["oxt"][:, n * 512:(n + 1) * 512] for n in range(4)]
                ov = emit_ln_norm(em, cst, rcs, cst["m5"], cst["m6"], outcs, s_v, s_a)
                em.wait("sync", s_v, ov)
                store_prev = em.dma("sync", io["x2s"][128 * tt:128 * (tt + 1), :],
                                    sb["oxt"], s_st)
            em.wait("tensor", s_v, em.val(s_v))
    em.wait("sync", s_st)
    em.wait("gpsimd", s_st)
    em.wait("vector", s_st)


def alloc_l0f(S, sb):
    sb["arenaA"] = S("arenaA", [128, ND * 2048], BF16)
    sb["hbuf"] = S("hbuf", [128, 44 * 1024], BF16)
    sb["gbuf"] = [S("gbuf0", [128, 4096], BF16), S("gbuf1", [128, 4096], BF16)]
    sb["silu_sb"] = S("silu_sb", [128, 1024], BF16)
    sb["bdbuf"] = [S("bdbuf%d" % i, [128, 2048], BF16) for i in range(3)]
    sb["rtile"] = S("rtile", [128, 4096], F32)
    sb["x1c"] = [S("x1c0", [128, 512], BF16), S("x1c1", [128, 512], BF16)]
    sb["psums"] = S("psums", [128, 4], F32)
    sb["psq"] = S("psq", [128, 4], F32)
    sb["sqscr"] = S("sqscr", [128, 512], BF16)
    sb["oxt"] = S("oxt", [128, 2048], BF16)


IO_L0F = dict(
    x1s=((TPC, D), BF16, "ExternalInput"),
    wgu_t=((88, 128, 16, 128), BF16, "ExternalInput"),
    wdn=((I_GU, D), BF16, "ExternalInput"),
    x2s=((TPC, D), BF16, "ExternalOutput"),
)


def emit_l1a(em, nc, sb, io, psA, psB, stop_after=None):
    s_tp = em.sem("t_tp", "s")
    s_w = [em.sem("t_w0", "g"), em.sem("t_w1", "g"), em.sem("t_w2", "g")]
    s_cs = [em.sem("t_cs0", "s"), em.sem("t_cs1", "s")]
    s_kv = em.sem("t_kv", "s")
    s_cc = em.sem("t_cc", "cc")
    s_kt = [em.sem("t_kt0", "s"), em.sem("t_kt1", "s")]
    s_q = [em.sem("t_q0", "s"), em.sem("t_q1", "s")]
    s_xc = [em.sem("t_xc0", "s"), em.sem("t_xc1", "s")]
    s_mm = em.sem("t_mm")
    s_v = em.sem("t_v")
    s_a = em.sem("t_a")
    s_st = em.sem("t_st", "s")

    cst = sb["cst"]
    x2T = sb["arenaA"]

    def xtile(k):
        return x2T[:, 2048 * k:2048 * (k + 1)]

    for k in range(ND):
        em.dma_tp("sync", xtile(k), io["x2s"][:, 128 * k:128 * (k + 1)], s_tp)
    em.wait("tensor", s_tp, em.val(s_tp))

    em.dma("gpsimd", sb["wkv"], io["wkv_b"], s_w[0])
    em.wait("tensor", s_w[0], em.val(s_w[0]))

    rope_prev = 0
    for t in range(NT):
        ps = (psA if t % 2 == 0 else psB)[:, 0:1024]
        cs = sb["csk"][t % 2]
        em.wait("sync", s_v, rope_prev)
        em.wait("sync", s_cs[t % 2], em.val(s_cs[t % 2]))
        em.dma("sync", cs[:, 0:512], io["cosk"][128 * t:128 * (t + 1), :], s_cs[t % 2])
        em.dma("sync", cs[:, 512:1024], io["sink"][128 * t:128 * (t + 1), :], s_cs[t % 2])
        em.wait("tensor", s_v, rope_prev)
        for k in range(ND):
            em.mm(ps[:, 0:512], xtile(k)[:, 128 * t:128 * (t + 1)],
                  sb["wkv"][:, 1024 * k:1024 * k + 512],
                  start=(k == 0), stop=(k == ND - 1), sem=None)
            em.mm(ps[:, 512:1024], xtile(k)[:, 128 * t:128 * (t + 1)],
                  sb["wkv"][:, 1024 * k + 512:1024 * (k + 1)],
                  start=(k == 0), stop=(k == ND - 1),
                  sem=s_mm if k == ND - 1 else None)
        em.wait("vector", s_mm)
        em.wait("vector", s_cs[t % 2], em.val(s_cs[t % 2]))
        em.wait("vector", s_kv, em.val(s_kv))
        vv = em.op("vector", "tensor_copy", sem=s_v, out=sb["kvout"][:, 512:1024],
                   in_=ps[:, 512:1024])
        em.op("vector", "tensor_tensor", sem=s_v, out=sb["ropet"][:, 0:512],
              in0=ps[:, 0:512], in1=cs[:, 0:512], op=OP.mult)
        strip_i = ps[:, 0:512].rearrange("p (h c) -> p h c", h=4)[:, :, 0:127]
        strip_s = cs[:, 512:1024].rearrange("p (h c) -> p h c", h=4)[:, :, 1:128]
        strip_o = sb["ropet"][:, 512:1024].rearrange("p (h c) -> p h c", h=4)[:, :, 1:128]
        em.op("vector", "tensor_tensor", sem=s_v, out=strip_o, in0=strip_i,
              in1=strip_s, op=OP.mult)
        col_i = ps[:, 0:512].rearrange("p (h c) -> p h c", h=4)[:, :, 127:128]
        col_s = cs[:, 512:1024].rearrange("p (h c) -> p h c", h=4)[:, :, 0:1]
        col_o = sb["ropet"][:, 512:1024].rearrange("p (h c) -> p h c", h=4)[:, :, 0:1]
        em.op("vector", "tensor_tensor", sem=s_v, out=col_o, in0=col_i,
              in1=col_s, op=OP.mult)
        em.wait("vector", s_v)
        rv = em.op("vector", "tensor_tensor", sem=s_v, out=sb["kvout"][:, 0:512],
                   in0=sb["ropet"][:, 0:512], in1=sb["ropet"][:, 512:1024], op=OP.add)
        rope_prev = em.val(s_v)
        em.wait("sync", s_v, rope_prev)
        em.dma("sync", io["cck_in"][128 * t:128 * (t + 1), :], sb["kvout"][:, 0:512], s_kv)
        em.dma("sync", io["ccv_in"][128 * t:128 * (t + 1), :], sb["kvout"][:, 512:1024], s_kv)
    kv_done = em.val(s_kv)

    em.wait("gpsimd", s_kv, kv_done)
    em.op("gpsimd", "collective_compute", sem=s_cc, kind="AllGather",
          op=OP.bypass, replica_groups=[[0, 1], [2, 3], [4, 5], [6, 7]],
          ins=[io["cck_in"][:, :]], outs=[io["cck_out"][:, :]])
    em.op("gpsimd", "collective_compute", sem=s_cc, kind="AllGather",
          op=OP.bypass, replica_groups=[[0, 1], [2, 3], [4, 5], [6, 7]],
          ins=[io["ccv_in"][:, :]], outs=[io["ccv_out"][:, :]])
    cc_done = em.val(s_cc)
    em.wait("sync", s_cc, cc_done)

    for t in range(NT):
        em.dma("sync", sb["cosq"][:, 128 * t:128 * (t + 1)],
               io["cosq"][128 * t:128 * (t + 1), :], s_tp)
        em.dma("sync", sb["sinq"][:, 128 * t:128 * (t + 1)],
               io["sinq"][128 * t:128 * (t + 1), :], s_tp)
    csq_ld = em.val(s_tp)

    NKT = T // 128
    store_prev = 0
    ktv_mm = [0, 0]
    qh_mm = [0, 0]
    w_mm = [0, 0, 0]
    group_cur = [-1]
    exp_prev = [0, 0]
    for qh in range(2):
        qev_prev = 0
        for tp in range(4):
            for k in range(ND):
                wb = sb["wband"][k % 3]
                em.wait("gpsimd", s_mm, w_mm[k % 3])
                em.wait("gpsimd", s_w[k % 3], em.val(s_w[k % 3]))
                em.dma("gpsimd", wb, io["wq"].rearrange("(kk p) o -> kk p o", p=128)[k],
                       s_w[k % 3])
                em.wait("tensor", s_w[k % 3], em.val(s_w[k % 3]))
                if k == 0:
                    em.wait("tensor", s_v, qev_prev)
                for t in range(2):
                    tt = 8 * qh + 2 * tp + t
                    ps = psA if t == 0 else psB
                    for n in range(4):
                        em.mm(ps[:, 512 * n:512 * (n + 1)],
                              xtile(k)[:, 128 * tt:128 * (tt + 1)],
                              wb[:, 512 * n:512 * (n + 1)],
                              start=(k == 0), stop=(k == ND - 1),
                              sem=s_mm if (t == 1 and n == 3) else None,
                              force=(t == 1 and n == 3))
                w_mm[k % 3] = em.val(s_mm)
            qp_mm = em.val(s_mm)
            em.wait("vector", s_mm, qp_mm)
            em.wait("vector", s_tp, csq_ld)
            for t in range(2):
                tt = 8 * qh + 2 * tp + t
                ps = psA if t == 0 else psB
                em.wait("vector", s_kv, em.val(s_kv))
                cc_ = sb["cosq"][:, 128 * tt:128 * tt + 128]
                ss_ = sb["sinq"][:, 128 * tt:128 * tt + 128]
                ccb = cc_.unsqueeze(1).broadcast_to([128, 16, 128])
                ssb = ss_.unsqueeze(1).broadcast_to([128, 16, 128])
                qv = ps[:, :].rearrange("p (h c) -> p h c", h=16)
                ro = sb["ropeq"].rearrange("p (h c) -> p h c", h=16)
                em.op("vector", "tensor_tensor", sem=s_v, out=ro, in0=qv,
                      in1=ccb, op=OP.mult)
                em.op("vector", "tensor_tensor", sem=s_v,
                      out=sb["ropeq2"].rearrange("p (h c) -> p h c", h=16)[:, :, 1:128],
                      in0=qv[:, :, 0:127], in1=ssb[:, :, 1:128], op=OP.mult)
                em.op("vector", "tensor_tensor", sem=s_v,
                      out=sb["ropeq2"].rearrange("p (h c) -> p h c", h=16)[:, :, 0:1],
                      in0=qv[:, :, 127:128], in1=ssb[:, :, 0:1], op=OP.mult)
                em.wait("vector", s_v)
                qv2 = em.op("vector", "tensor_tensor", sem=s_v, out=sb["ropeq"],
                            in0=sb["ropeq"], in1=sb["ropeq2"], op=OP.add)
                em.wait("sync", s_v, qv2)
                em.dma("sync", io["qscr"][128 * (tt % 8):128 * (tt % 8) + 128, :],
                       sb["ropeq"], s_kv)
            qev_prev = em.val(s_v)
        em.wait("sync", s_kv)
        qscr_done = em.val(s_kv)

        for h in range(H1):
            g = h // 4
            hb = sb["qT"][h % 2]
            em.wait("sync", s_mm, qh_mm[h % 2])
            em.wait("sync", s_q[h % 2], em.val(s_q[h % 2]))
            em.dma_tp("sync", hb, io["qscr"][:, 128 * h:128 * (h + 1)], s_q[h % 2])
            if g != group_cur[0]:
                em.wait("sync", s_mm, em.val(s_mm))
                em.wait("sync", s_kt[0], em.val(s_kt[0]))
                em.dma_tp("sync", sb["kTg"], io["cck_out"][:, 128 * g:128 * (g + 1)],
                          s_kt[0])
                for kt in range(NKT):
                    em.dma("sync", sb["vg"][:, 128 * kt:128 * (kt + 1)],
                           io["ccv_out"][128 * kt:128 * (kt + 1),
                                         128 * g:128 * (g + 1)], s_kt[1])
                group_cur[0] = g
            em.wait("tensor", s_q[h % 2], em.val(s_q[h % 2]))
            em.wait("tensor", s_kt[0], em.val(s_kt[0]))
            em.wait("tensor", s_kt[1], em.val(s_kt[1]))
            for kt in range(NKT):
                pk = kt % 2
                psS = psA[:, 1024 * pk:1024 * (pk + 1)]
                em.wait("tensor", s_a, exp_prev[pk])
                for n in range(2):
                    em.mm(psS[:, 512 * n:512 * (n + 1)],
                          sb["kTg"][:, 128 * kt:128 * (kt + 1)],
                          hb[:, 512 * n:512 * (n + 1)],
                          start=True, stop=True,
                          sem=s_mm if n == 1 else None, force=(n == 1))
                em.wait("scalar", s_mm, em.val(s_mm))
                pb = sb["pbuf"][pk]
                em.op("scalar", "activation", sem=s_a, out=pb, in_=psS,
                      func=AF.Exp, scale=cst["cqscale"], bias=cst["cqbias"])
                exp_prev[pk] = em.val(s_a)
                em.wait("tensor", s_a, exp_prev[pk])
                em.mm(psB[:, 0:512], sb["vg"][:, 128 * kt:128 * (kt + 1)],
                      pb[:, 0:512], start=(kt == 0), stop=(kt == NKT - 1), sem=None)
                em.mm(psB[:, 512:1024], sb["vg"][:, 128 * kt:128 * (kt + 1)],
                      pb[:, 512:1024], start=(kt == 0), stop=(kt == NKT - 1), sem=None)
                em.mm(psB[:, 1024:1536], sb["ones"], pb[:, 0:512],
                      start=(kt == 0), stop=(kt == NKT - 1), sem=None)
                em.mm(psB[:, 1536:2048], sb["ones"], pb[:, 512:1024],
                      start=(kt == 0), stop=(kt == NKT - 1),
                      sem=s_mm if kt == NKT - 1 else None)
            av_mm = em.val(s_mm)
            em.wait("scalar", s_mm, av_mm)
            v1 = em.op("scalar", "activation", sem=s_a, out=sb["rec"],
                       in_=psB[:, 1024:2048], func=AF.Ln)
            em.wait("scalar", s_a, v1)
            v2 = em.op("scalar", "activation", sem=s_a, out=sb["rec"],
                       in_=sb["rec"], func=AF.Exp, scale=cst["cm1"])
            em.wait("vector", s_a, v2)
            em.wait("vector", s_v)
            avv = em.op("vector", "tensor_tensor", sem=s_v,
                        out=sb["avT"][:, 1024 * h:1024 * (h + 1)],
                        in0=psB[:, 0:1024], in1=sb["rec"], op=OP.mult)
            qh_mm[h % 2] = em.val(s_mm)
            em.wait("tensor", s_v, em.val(s_v))
        att_done = em.val(s_v)

        em.wait("tensor", s_v, att_done)
        for tp in range(4):
            for k in range(H1):
                wb = sb["wband"][k % 3]
                em.wait("gpsimd", s_mm, w_mm[k % 3])
                em.wait("gpsimd", s_w[k % 3], em.val(s_w[k % 3]))
                em.dma("gpsimd", wb, io["wo"].rearrange("(kk p) o -> kk p o", p=128)[k],
                       s_w[k % 3])
                em.wait("tensor", s_w[k % 3], em.val(s_w[k % 3]))
                for t in range(2):
                    ps = psA if t == 0 else psB
                    for n in range(4):
                        em.mm(ps[:, 512 * n:512 * (n + 1)],
                              sb["avT"][:, 1024 * k + 128 * (2 * tp + t):
                                        1024 * k + 128 * (2 * tp + t + 1)],
                              wb[:, 512 * n:512 * (n + 1)],
                              start=(k == 0), stop=(k == H1 - 1),
                              sem=s_mm if (t == 1 and n == 3) else None,
                              force=(t == 1 and n == 3))
                w_mm[k % 3] = em.val(s_mm)
            wo_mm = em.val(s_mm)
            em.wait("vector", s_mm, wo_mm)
            for t in range(2):
                tt = 8 * qh + 2 * tp + t
                ps = psA if t == 0 else psB
                for n in range(4):
                    xc = sb["x2c"][n % 2]
                    em.wait("sync", s_v, em.val(s_v))
                    em.dma("sync", xc,
                           io["x2s"][128 * tt:128 * (tt + 1), 512 * n:512 * (n + 1)],
                           s_xc[n % 2])
                    em.wait("vector", s_xc[n % 2], em.val(s_xc[n % 2]))
                    rc = sb["rtile"][:, 2048 * t + 512 * n:2048 * t + 512 * (n + 1)]
                    em.op("vector", "scalar_tensor_tensor", sem=s_v, out=rc,
                          in0=xc, scalar=1.0, in1=ps[:, 512 * n:512 * (n + 1)],
                          op0=OP.mult, op1=OP.add,
                          accum_out=sb["psums"][:, n:n + 1])
                em.wait("vector", s_v)
                em.wait("vector", s_st, store_prev)
                em.op("vector", "tensor_tensor", sem=s_v, out=cst["m5"],
                      in0=sb["psums"][:, 0:1], in1=sb["psums"][:, 1:2], op=OP.add)
                em.op("vector", "tensor_tensor", sem=s_v, out=cst["m6"],
                      in0=sb["psums"][:, 2:3], in1=sb["psums"][:, 3:4], op=OP.add)
                em.wait("vector", s_v)
                v = em.op("vector", "tensor_tensor", sem=s_v, out=cst["m5"],
                          in0=cst["m5"], in1=cst["m6"], op=OP.add)
                em.wait("vector", s_v, v)
                rcs = [sb["rtile"][:, 2048 * t + 512 * n:2048 * t + 512 * (n + 1)]
                       for n in range(4)]
                em.wait("scalar", s_v)
                for n in range(4):
                    em.wait("scalar", s_a, em.val(s_a))
                    em.op("scalar", "activation", sem=s_a, out=sb["sqscr"],
                          in_=rcs[n], func=AF.Square, accum_out=sb["psq"][:, n:n + 1])
                em.wait("vector", s_a)
                em.op("vector", "tensor_tensor", sem=s_v, out=cst["m6"],
                      in0=sb["psq"][:, 0:1], in1=sb["psq"][:, 1:2], op=OP.add)
                em.op("vector", "tensor_tensor", sem=s_v, out=cst["m2"],
                      in0=sb["psq"][:, 2:3], in1=sb["psq"][:, 3:4], op=OP.add)
                em.wait("vector", s_v)
                v = em.op("vector", "tensor_tensor", sem=s_v, out=cst["m6"],
                          in0=cst["m6"], in1=cst["m2"], op=OP.add)
                em.wait("vector", s_v, v)
                outcs = [sb["oxt"][:, n * 512:(n + 1) * 512] for n in range(4)]
                ov = emit_ln_norm(em, cst, rcs, cst["m5"], cst["m6"], outcs, s_v, s_a)
                em.wait("sync", s_v, ov)
                store_prev = em.dma("sync", io["x3s"][128 * tt:128 * (tt + 1), :],
                                    sb["oxt"], s_st)
            em.wait("tensor", s_v, em.val(s_v))
    em.wait("sync", s_st)
    em.wait("gpsimd", s_st)
    em.wait("vector", s_st)


def alloc_l1a(S, sb):
    sb["arenaA"] = S("arenaA", [128, ND * 2048], BF16)
    sb["wkv"] = S("wkv", [128, 16 * 1024], BF16)
    sb["csk"] = [S("csk0", [128, 1024], F32), S("csk1", [128, 1024], F32)]
    sb["kvout"] = S("kvout", [128, 1024], BF16)
    sb["ropet"] = S("ropet", [128, 1024], F32)
    sb["cosq"] = S("cosq_sb", [128, 16 * 128], F32)
    sb["sinq"] = S("sinq_sb", [128, 16 * 128], F32)
    sb["wband"] = [S("wband%d" % i, [128, 2048], BF16) for i in range(3)]
    sb["ropeq"] = S("ropeq", [128, 2048], BF16)
    sb["ropeq2"] = S("ropeq2", [128, 2048], BF16)
    wkv = sb["wkv"]
    sb["kTg"] = wkv[:, 0:4096]
    sb["vg"] = wkv[:, 4096:8192]
    sb["qT"] = [wkv[:, 8192:9216], wkv[:, 9216:10240]]
    sb["pbuf"] = [wkv[:, 10240:11264], wkv[:, 11264:12288]]
    sb["rec"] = wkv[:, 12288:14336].bitcast(F32)
    sb["avT"] = S("avT", [128, 16 * 1024], BF16)
    sb["ones"] = S("ones", [128, 128], BF16)
    sb["rtile"] = S("rtile", [128, 4096], F32)
    sb["x2c"] = [S("x2c0", [128, 512], BF16), S("x2c1", [128, 512], BF16)]
    sb["psums"] = S("psums", [128, 4], F32)
    sb["psq"] = S("psq", [128, 4], F32)
    sb["sqscr"] = S("sqscr", [128, 512], BF16)
    sb["oxt"] = S("oxt", [128, 2048], BF16)


IO_L1A = dict(
    x2s=((TPC, D), BF16, "ExternalInput"),
    wkv_b=((128, 16 * 1024), F32, "ExternalInput"),
    wq=((D, D), F32, "ExternalInput"),
    wo=((D, D), F32, "ExternalInput"),
    cosk=((TPC, 512), F32, "ExternalInput"),
    sink=((TPC, 512), F32, "ExternalInput"),
    cosq=((TPC, 128), F32, "ExternalInput"),
    sinq=((TPC, 128), F32, "ExternalInput"),
    x3s=((TPC, D), BF16, "ExternalOutput"),
    cck_in=((TPC, 512), BF16, "Internal"),
    ccv_in=((TPC, 512), BF16, "Internal"),
    cck_out=((2 * TPC, 512), BF16, "Internal"),
    ccv_out=((2 * TPC, 512), BF16, "Internal"),
    qscr=((1024, D), BF16, "Internal"),
)


def emit_moe(em, nc, sb, io, psA, psB, stop_after=None, n_experts=E_MOE):
    s_tp = em.sem("m_tp", "s")
    s_g = [em.sem("m_g0", "g"), em.sem("m_g1", "g")]
    s_d = [em.sem("m_d0", "g"), em.sem("m_d1", "g"), em.sem("m_d2", "g")]
    s_xc = [em.sem("m_xc0", "s"), em.sem("m_xc1", "s")]
    s_so = [em.sem("m_so0", "s"), em.sem("m_so1", "s")]
    s_mm = em.sem("m_mm")
    s_v = em.sem("m_v")
    s_a = em.sem("m_a")
    s_st = em.sem("m_st", "s")

    cst = sb["cst"]
    x3T = sb["arenaA"]
    acc = sb["acc"]
    hbuf = sb["hbuf"]

    def xtile(k):
        return x3T[:, 2048 * k:2048 * (k + 1)]

    em.dma("sync", sb["identf"], io["identf_in"], s_tp)
    em.dma("sync", sb["oneh"], io["onehots"], s_tp)
    for k in range(ND):
        em.dma_tp("sync", xtile(k), io["x3s"][:, 128 * k:128 * (k + 1)], s_tp)
    em.wait("tensor", s_tp, em.val(s_tp))

    em.dma("gpsimd", sb["gw"], io["gatew_b"], s_g[0])
    em.wait("tensor", s_g[0], em.val(s_g[0]))
    cT = sb["cT"]
    for t in range(NT):
        ps = psA[:, 0:16]
        em.wait("tensor", s_v, em.val(s_v))
        for k in range(ND):
            em.mm(ps, xtile(k)[:, 128 * t:128 * (t + 1)],
                  sb["gw"][:, 16 * k:16 * (k + 1)],
                  start=(k == 0), stop=(k == ND - 1), sem=s_mm if k == ND - 1 else None)
        em.wait("vector", s_mm)
        sc = sb["rsc"]
        v = em.op("vector", "tensor_copy", sem=s_v, out=sc, in_=ps)
        em.wait("vector", s_v, v)
        v = em.op("vector", "reduce_max", sem=s_v, out=cst["m1"], in_=sc, axis=AX.X)
        em.wait("vector", s_v, v)
        v = em.op("vector", "tensor_single_scalar", sem=s_v, out=cst["m1"],
                  in_=cst["m1"], scalar=-1.0, op=OP.mult)
        em.wait("scalar", s_v, v)
        v = em.op("scalar", "activation", sem=s_a, out=sb["rpr"], in_=sc,
                  func=AF.Exp, bias=cst["m1"])
        em.wait("vector", s_a, v)
        v = em.op("vector", "reduce_sum", sem=s_v, out=cst["m2"], in_=sb["rpr"],
                  axis=AX.X)
        em.wait("vector", s_v, v)
        v = em.op("vector", "reciprocal", sem=s_v, out=cst["m2"], in_=cst["m2"])
        em.wait("vector", s_v, v)
        v = em.op("vector", "tensor_scalar_mul", sem=s_v, out=sb["rpr"],
                  in0=sb["rpr"], scalar1=cst["m2"])
        em.wait("vector", s_v, v)
        v = em.op("vector", "tensor_copy", sem=s_v, out=sb["rwk"], in_=sb["rpr"])
        em.wait("vector", s_v, v)
        for it6 in range(TOPK):
            v = em.op("vector", "reduce_max", sem=s_v, out=cst["m1"], in_=sb["rwk"],
                      axis=AX.X)
            em.wait("vector", s_v, v)
            v = em.op("vector", "tensor_single_scalar", sem=s_v, out=sb["rms"],
                      in_=sb["rwk"], scalar=cst["m1"], op=OP.is_ge)
            em.wait("vector", s_v, v)
            v = em.op("vector", "scalar_tensor_tensor", sem=s_v, out=sb["rwk"],
                      in0=sb["rms"], scalar=-2.0, in1=sb["rwk"], op0=OP.mult,
                      op1=OP.add)
            em.wait("vector", s_v, v)
        v = em.op("vector", "tensor_single_scalar", sem=s_v, out=sb["rms"],
                  in_=sb["rwk"], scalar=-1.0, op=OP.is_le)
        em.wait("vector", s_v, v)
        v = em.op("vector", "tensor_tensor", sem=s_v, out=sb["rms"], in0=sb["rms"],
                  in1=sb["rpr"], op=OP.mult)
        em.wait("vector", s_v, v)
        v = em.op("vector", "reduce_sum", sem=s_v, out=cst["m2"], in_=sb["rms"],
                  axis=AX.X)
        em.wait("vector", s_v, v)
        v = em.op("vector", "tensor_single_scalar", sem=s_v, out=cst["m2"],
                  in_=cst["m2"], scalar=1e-20, op=OP.add)
        em.wait("vector", s_v, v)
        v = em.op("vector", "reciprocal", sem=s_v, out=cst["m2"], in_=cst["m2"])
        em.wait("vector", s_v, v)
        v = em.op("vector", "tensor_scalar_mul", sem=s_v, out=sb["rms"],
                  in0=sb["rms"], scalar1=cst["m2"])
        em.wait("tensor", s_v, v)
        tv = em.op("tensor", "transpose", sem=s_mm, out=psB[0:16, 1024:1152],
                   in_=sb["rms"], identity=sb["identf"])
        em.wait("vector", s_mm, tv)
        v = em.op("vector", "tensor_copy", sem=s_v, out=cT[:, 128 * t:128 * (t + 1)],
                  in_=psB[0:16, 1024:1152])
        em.wait("tensor", s_v, v)
    ct_done = em.val(s_v)
    em.wait("vector", s_v, ct_done)
    ctb_v = em.op("vector", "tensor_copy", sem=s_v, out=sb["cTb"], in_=cT)
    em.wait("tensor", s_v, ctb_v)

    NIT = I_MOE // 128
    NSH = IS_MOE // 128
    g_mm = [0, 0]
    d_mm = [0, 0, 0]
    cb_mm = [0, 0]
    ox_store = [0, 0]
    for th in range(2):
        t0 = th * 1024
        for e in range(n_experts + 2):
            shared = (e >= n_experts)
            shj = e - n_experts
            nit = NIT
            cb = sb["cbc"][e % 2]
            if shared:
                em.wait("vector", s_v, em.val(s_v))
                v = em.op("vector", "memset", sem=s_v, ap=cb, constant=1.0)
                em.wait("vector", s_v, v)
                cb_ready = ("v", em.val(s_v))
            else:
                em.wait("tensor", s_v, em.val(s_v))
                em.mm(psB[:, 1024:1536], sb["oneh"][:, 128 * e:128 * (e + 1)],
                      sb["cTb"][:, t0:t0 + 512], start=True, stop=True, sem=None)
                bv = em.mm(psB[:, 1536:2048], sb["oneh"][:, 128 * e:128 * (e + 1)],
                           sb["cTb"][:, t0 + 512:t0 + 1024], start=True, stop=True,
                           sem=s_mm)
                em.wait("vector", s_mm, bv)
                v = em.op("vector", "tensor_copy", sem=s_v, out=cb,
                          in_=psB[:, 1024:2048])
                cb_ready = ("v", em.val(s_v))
            tt_done = [cb_ready[1], cb_ready[1]]
            silu_done = 0
            for it in range(nit):
                gb = sb["gbuf"][it % 2]
                if shared:
                    gsrc = io["sw13_t"][11 * shj + it]
                    usrc = io["sw13_t"][22 + 11 * shj + it]
                else:
                    gsrc = io["w13_t"][e][it]
                    usrc = io["w13_t"][e][it + nit]
                em.wait("gpsimd", s_mm, g_mm[it % 2])
                em.wait("gpsimd", s_g[it % 2], em.val(s_g[it % 2]))
                em.dma("gpsimd", gb[:, 0:2048], gsrc, s_g[it % 2])
                em.wait("gpsimd", s_g[it % 2], em.val(s_g[it % 2]))
                em.dma("gpsimd", gb[:, 2048:4096], usrc, s_g[it % 2])
                em.wait("tensor", s_g[it % 2], em.val(s_g[it % 2]))
                psg = psA if it % 2 == 0 else psB
                em.wait("tensor", s_v, tt_done[it % 2])
                for k in range(ND):
                    for n in range(2):
                        em.mm(psg[:, 512 * n:512 * (n + 1)],
                              gb[:, 128 * k:128 * (k + 1)],
                              xtile(k)[:, t0 + 512 * n:t0 + 512 * (n + 1)],
                              start=(k == 0), stop=(k == ND - 1),
                              sem=s_mm if (k == ND - 1 and n == 1) else None,
                              force=(k == ND - 1 and n == 1))
                    for n in range(2):
                        em.mm(psg[:, 1024 + 512 * n:1024 + 512 * (n + 1)],
                              gb[:, 2048 + 128 * k:2048 + 128 * (k + 1)],
                              xtile(k)[:, t0 + 512 * n:t0 + 512 * (n + 1)],
                              start=(k == 0), stop=(k == ND - 1),
                              sem=s_mm if (k == ND - 1 and n == 1) else None,
                              force=(k == ND - 1 and n == 1))
                g_mm[it % 2] = em.val(s_mm)
                em.wait("scalar", s_mm, em.val(s_mm))
                em.wait("scalar", s_v, em.val(s_v))
                sv = em.op("scalar", "activation", sem=s_a, out=sb["silu_sb"],
                           in_=psg[:, 0:1024], func=AF.Silu)
                silu_done = sv
                em.wait("vector", s_v, cb_ready[1])
                em.wait("vector", s_a, sv)
                v = em.op("vector", "tensor_tensor", sem=s_v, out=sb["ucbuf"],
                          in0=psg[:, 1024:2048], in1=cb, op=OP.mult)
                em.wait("vector", s_v, v)
                em.op("vector", "tensor_tensor", sem=s_v, out=hbuf[:, 1024 * it:1024 * (it + 1)],
                      in0=sb["silu_sb"], in1=sb["ucbuf"], op=OP.mult)
                tt_done[it % 2] = em.val(s_v)
            h_done = em.val(s_v)
            cb_mm[e % 2] = em.val(s_mm)

            em.wait("tensor", s_v, h_done)
            acc_prev = 0
            for ht in range(ND):
                if shared:
                    dsrc = io["sw2_t"][ht][:, 1408 * shj:1408 * (shj + 1)]
                else:
                    dsrc = io["w2_t"][e][ht]
                db = sb["dbuf"][ht % 3]
                em.wait("gpsimd", s_mm, d_mm[ht % 3])
                em.wait("gpsimd", s_d[ht % 3], em.val(s_d[ht % 3]))
                em.dma("gpsimd", db[:, 0:128 * nit], dsrc, s_d[ht % 3])
                em.wait("tensor", s_d[ht % 3], em.val(s_d[ht % 3]))
                psd = (psA if ht % 2 == 0 else psB)[:, 0:1024]
                em.wait("tensor", s_v, acc_prev)
                for k in range(nit):
                    for n in range(2):
                        em.mm(psd[:, 512 * n:512 * (n + 1)],
                              db[:, 128 * k:128 * (k + 1)],
                              hbuf[:, 1024 * k + 512 * n:1024 * k + 512 * (n + 1)],
                              start=(k == 0), stop=(k == nit - 1),
                              sem=s_mm if (k == nit - 1 and n == 1) else None,
                              force=(k == nit - 1 and n == 1))
                d_mm[ht % 3] = em.val(s_mm)
                em.wait("vector", s_mm, em.val(s_mm))
                aslot = acc[:, 1024 * ht:1024 * (ht + 1)]
                if e == 0:
                    em.op("vector", "tensor_copy", sem=s_v, out=aslot, in_=psd)
                else:
                    em.op("vector", "tensor_tensor", sem=s_v, out=aslot,
                          in0=psd, in1=aslot, op=OP.add)
                acc_prev = em.val(s_v) if ht % 2 == 1 else acc_prev
            em.wait("tensor", s_v, em.val(s_v))

        ep_mm = em.val(s_mm)
        for t in range(8):
            tt = 8 * th + t
            em.wait("tensor", s_v, em.val(s_v))
            for ht in range(ND):
                v = em.op("tensor", "transpose", sem=s_mm,
                          out=(psA if ht % 2 == 0 else psB)[:, 1024 + 128 * ((ht // 2) % 4):1152 + 128 * ((ht // 2) % 4)],
                          in_=acc[:, 1024 * ht + 128 * t:1024 * ht + 128 * (t + 1)],
                          identity=sb["identf"])
                em.wait("vector", s_mm, v)
                em.op("vector", "tensor_copy", sem=s_v,
                      out=sb["rtile"][:, 128 * ht:128 * (ht + 1)],
                      in_=(psA if ht % 2 == 0 else psB)[:, 1024 + 128 * ((ht // 2) % 4):1152 + 128 * ((ht // 2) % 4)])
                em.wait("tensor", s_v, em.val(s_v))
            tr_done = em.val(s_v)
            em.wait("vector", s_v, tr_done)
            for n in range(4):
                xc = sb["x3c"][n % 2]
                em.wait("sync", s_v, em.val(s_v))
                em.dma("sync", xc,
                       io["x3s"][128 * tt:128 * (tt + 1), 512 * n:512 * (n + 1)],
                       s_xc[n % 2])
                em.wait("vector", s_xc[n % 2], em.val(s_xc[n % 2]))
                rc = sb["rtile"][:, 512 * n:512 * (n + 1)]
                em.op("vector", "scalar_tensor_tensor", sem=s_v, out=rc,
                      in0=xc, scalar=1.0, in1=rc, op0=OP.mult, op1=OP.add,
                      accum_out=sb["psums"][:, n:n + 1])
            em.wait("vector", s_v)
            em.op("vector", "tensor_tensor", sem=s_v, out=cst["m5"],
                  in0=sb["psums"][:, 0:1], in1=sb["psums"][:, 1:2], op=OP.add)
            em.op("vector", "tensor_tensor", sem=s_v, out=cst["m6"],
                  in0=sb["psums"][:, 2:3], in1=sb["psums"][:, 3:4], op=OP.add)
            em.wait("vector", s_v)
            v = em.op("vector", "tensor_tensor", sem=s_v, out=cst["m5"],
                      in0=cst["m5"], in1=cst["m6"], op=OP.add)
            em.wait("vector", s_v, v)
            rcs = [sb["rtile"][:, 512 * n:512 * (n + 1)] for n in range(4)]
            em.wait("scalar", s_v)
            for n in range(4):
                em.wait("scalar", s_a, em.val(s_a))
                em.op("scalar", "activation", sem=s_a, out=sb["sqscr"],
                      in_=rcs[n], func=AF.Square, accum_out=sb["psq"][:, n:n + 1])
            em.wait("vector", s_a)
            em.op("vector", "tensor_tensor", sem=s_v, out=cst["m6"],
                  in0=sb["psq"][:, 0:1], in1=sb["psq"][:, 1:2], op=OP.add)
            em.op("vector", "tensor_tensor", sem=s_v, out=cst["m2"],
                  in0=sb["psq"][:, 2:3], in1=sb["psq"][:, 3:4], op=OP.add)
            em.wait("vector", s_v)
            v = em.op("vector", "tensor_tensor", sem=s_v, out=cst["m6"],
                      in0=cst["m6"], in1=cst["m2"], op=OP.add)
            em.wait("vector", s_v, v)
            mean, var, t3, rstd = cst["m1"], cst["m2"], cst["m3"], cst["m4"]
            em.wait("vector", s_v)
            v = em.op("vector", "tensor_single_scalar", sem=s_v, out=mean,
                      in_=cst["m5"], scalar=1.0 / D, op=OP.mult)
            em.wait("vector", s_v, v)
            v = em.op("vector", "tensor_tensor", sem=s_v, out=var, in0=mean,
                      in1=mean, op=OP.mult)
            em.wait("vector", s_v, v)
            v = em.op("vector", "tensor_single_scalar", sem=s_v, out=t3,
                      in_=cst["m6"], scalar=1.0 / D, op=OP.mult)
            em.wait("vector", s_v, v)
            v = em.op("vector", "tensor_tensor", sem=s_v, out=var, in0=t3,
                      in1=var, op=OP.subtract)
            em.wait("scalar", s_v, v)
            v = em.op("scalar", "activation", sem=s_a, out=t3, in_=var,
                      func=AF.Ln, bias=cst["ceps"])
            em.wait("scalar", s_a, v)
            v = em.op("scalar", "activation", sem=s_a, out=rstd, in_=t3,
                      func=AF.Exp, scale=cst["cmhalf"])
            em.wait("vector", s_a, v)
            for n in range(4):
                ox = sb["oxf"][n % 2]
                em.wait("vector", s_so[n % 2], ox_store[n % 2])
                v = em.op("vector", "tensor_scalar", sem=s_v, out=ox, in0=rcs[n],
                          scalar1=mean, scalar2=rstd, op0=OP.subtract, op1=OP.mult)
                em.wait("sync", s_v, v)
                ox_store[n % 2] = em.dma(
                    "sync", io["outs"][128 * tt:128 * (tt + 1), 512 * n:512 * (n + 1)],
                    ox, s_so[n % 2])
    em.wait("sync", s_so[0])
    em.wait("sync", s_so[1])
    em.wait("gpsimd", s_so[0])
    em.wait("gpsimd", s_so[1])


def alloc_moe(S, sb):
    sb["arenaA"] = S("arenaA", [128, ND * 2048], BF16)
    sb["acc"] = S("acc", [128, 16 * 1024], F32)
    sb["hbuf"] = S("hbuf", [128, 11 * 1024], BF16)
    sb["gbuf"] = [S("gbuf0", [128, 4096], BF16), S("gbuf1", [128, 4096], BF16)]
    sb["dbuf"] = [S("dbuf%d" % i, [128, 11 * 128], BF16) for i in range(3)]
    sb["silu_sb"] = S("silu_sb", [128, 1024], BF16)
    sb["ucbuf"] = S("ucbuf", [128, 1024], BF16)
    sb["cbc"] = [S("cbc0", [128, 1024], BF16), S("cbc1", [128, 1024], BF16)]
    sb["gw"] = S("gw", [128, 256], BF16)
    sb["oneh"] = S("oneh", [16, 16 * 128], BF16)
    sb["rsc"] = S("rsc", [128, 16], F32)
    sb["rpr"] = S("rpr", [128, 16], F32)
    sb["rwk"] = S("rwk", [128, 16], F32)
    sb["rms"] = S("rms", [128, 16], F32)
    sb["identf"] = S("identf", [128, 128], F32)
    sb["rtile"] = S("rtile", [128, 2048], F32)
    sb["cT"] = sb["rtile"][0:16, :]
    sb["cTb"] = S("cTb", [16, 2048], BF16)
    sb["x3c"] = [S("x3c0", [128, 512], BF16), S("x3c1", [128, 512], BF16)]
    sb["psums"] = S("psums", [128, 4], F32)
    sb["psq"] = S("psq", [128, 4], F32)
    sb["sqscr"] = S("sqscr", [128, 512], BF16)
    sb["oxf"] = [S("oxf0", [128, 512], F32), S("oxf1", [128, 512], F32)]


IO_MOE = dict(
    x3s=((TPC, D), BF16, "ExternalInput"),
    gatew_b=((128, 256), F32, "ExternalInput"),
    onehots=((16, 16 * 128), BF16, "ExternalInput"),
    w13_t=((E_MOE, 22, 128, D), F32, "ExternalInput"),
    w2_t=((E_MOE, 16, 128, I_MOE), F32, "ExternalInput"),
    sw13_t=((44, 128, D), F32, "ExternalInput"),
    sw2_t=((16, 128, IS_MOE), F32, "ExternalInput"),
    identf_in=((128, 128), F32, "ExternalInput"),
    outs=((TPC, D), F32, "ExternalOutput"),
)


def emit_dq(em, nc, sb, io):
    s_ld = em.sem("dq_ld", "s")
    s_v = em.sem("dq_v")
    s_st = em.sem("dq_st", "s")
    s_st2 = em.sem("dq_st2", "s")
    CH = 2816

    def decode(src, scl, width, j, wbuf, P=128):
        m3 = sb["dq_m3"][0:P, 0:width]
        ta = sb["dq_ta"][0:P, 0:width]
        fb = sb["dq_fb"][0:P, 0:width]
        sg = sb["dq_sg"][0:P, 0:width]
        wout = wbuf[0:P, 0:width]
        v0 = em.op("vector", "tensor_scalar", sem=s_v, out=m3, in0=src,
                   scalar1=4 * j, scalar2=7, op0=OP.logical_shift_right,
                   op1=OP.bitwise_and)
        em.op("vector", "tensor_scalar", sem=s_v, out=sg, in0=src,
              scalar1=12 - 4 * j, scalar2=0x8000, op0=OP.logical_shift_left,
              op1=OP.bitwise_and)
        em.wait("vector", s_v, v0)
        em.op("vector", "tensor_scalar", sem=s_v, out=ta, in0=m3,
              scalar1=64, scalar2=16128, op0=OP.mult, op1=OP.add)
        v1 = em.op("vector", "tensor_scalar", sem=s_v, out=fb, in0=m3,
                   scalar1=2, scalar2=16128, op0=OP.min, op1=OP.mult)
        em.wait("vector", s_v, v1)
        v2 = em.op("vector", "tensor_tensor", sem=s_v, out=fb, in0=fb, in1=ta,
                   op=OP.min)
        em.wait("vector", s_v, v2)
        v3 = em.op("vector", "tensor_tensor", sem=s_v, out=fb, in0=fb, in1=sg,
                   op=OP.bitwise_or)
        em.wait("vector", s_v, v3)
        return em.op("vector", "tensor_tensor", sem=s_v, out=wout,
                     in0=fb.bitcast(BF16), in1=scl, op=OP.mult)

    wbuf_st = [(0, 0), (0, 0)]
    it = 0
    for w in range(2):
        for pl in range(2):
            plane = io["gup_lo"] if pl == 0 else io["gup_hi"]
            for ch in range(4):
                c0 = ch * CH
                buf = sb["dq_src"][it % 2]
                scl = sb["dq_scl"][it % 2]
                em.wait("sync", s_v)
                em.dma("sync", buf[:, 0:CH],
                       plane[128 * w:128 * (w + 1), c0:c0 + CH], s_ld)
                em.dma("sync", scl[:, 0:CH], io["gup_scale"][w, :, c0:c0 + CH], s_ld)
                em.wait("vector", s_ld)
                for j in range(4):
                    wb = sb["dq_w"][it % 2]
                    em.wait("vector", s_st, wbuf_st[it % 2][0])
                    em.wait("vector", s_st2, wbuf_st[it % 2][1])
                    em.wait("vector", s_v)
                    dv = decode(buf[:, 0:CH], scl[:, 0:CH], CH, j, wb)
                    em.wait("sync", s_v, dv)
                    em.wait("sync", s_st, em.val(s_st))
                    em.wait("sync", s_st2, em.val(s_st2))
                    gv = io["wgu_t"].rearrange("ot (q j2) k c -> ot k q j2 c", j2=8)
                    for oti in range(22):
                        ot = ch * 22 + oti
                        em.dma("sync",
                               gv[ot, 8 * w:8 * w + 8, :, 4 * pl + j, :],
                               wb[:, oti * 128:(oti + 1) * 128],
                               s_st if oti < 11 else s_st2)
                    wbuf_st[it % 2] = (em.val(s_st), em.val(s_st2))
                    it += 1

    dnv = io["wdn"].rearrange("(i8 e) c -> i8 e c", e=8)
    for w in range(6):
        P = 128 if w < 5 else 64
        for pl in range(2):
            plane = io["dn_lo"] if pl == 0 else io["dn_hi"]
            buf = sb["dq_src"][it % 2]
            scl = sb["dq_scl"][it % 2]
            em.wait("sync", s_v)
            em.dma("sync", buf[0:P, 0:D], plane[128 * w:128 * w + P, :], s_ld)
            em.dma("sync", scl[0:P, 0:D], io["dn_scale"][w, 0:P, :], s_ld)
            em.wait("vector", s_ld)
            for j in range(4):
                wb = sb["dq_w"][it % 2]
                em.wait("vector", s_st, wbuf_st[it % 2][0])
                em.wait("vector", s_st2, wbuf_st[it % 2][1])
                em.wait("vector", s_v)
                dv = decode(buf[0:P, 0:D], scl[0:P, 0:D], D, j, wb, P=P)
                em.wait("sync", s_v, dv)
                em.wait("sync", s_st, em.val(s_st))
                em.dma("sync", dnv[128 * w:128 * w + P, 4 * pl + j, :],
                       wb[0:P, 0:D], s_st)
                wbuf_st[it % 2] = (em.val(s_st), em.val(s_st2))
                it += 1
    em.wait("sync", s_st)
    em.wait("sync", s_st2)


def _tile_emajor(wT, kdim, odim):
    w4 = wT.reshape(kdim // 128, 128, odim // 128, 128)
    return np.ascontiguousarray(w4.transpose(2, 1, 0, 3).reshape(odim // 128, 128, kdim))


def prep_inputs(inp):
    x0 = np.ascontiguousarray(inp["L0_attn_x"])

    winT = np.ascontiguousarray(np.transpose(inp["L0_attn_W_in"]))
    winT_t = _tile_emajor(winT, D, D)
    woutT = np.ascontiguousarray(np.transpose(inp["L0_attn_W_out"]))
    wdtT = np.transpose(inp["L0_attn_W_dt"])
    wdtT_r = np.ascontiguousarray(
        wdtT.reshape(16, 128, 32).transpose(1, 0, 2).reshape(128, 512))
    logA_t = np.ascontiguousarray(inp["L0_attn_log_A"].reshape(-1).reshape(16, 128).T)
    b_dt = np.ascontiguousarray(inp["L0_attn_b_dt"].reshape(32, 1))

    gup = np.ascontiguousarray(np.transpose(inp["L0_ffn_gate_up_packed"]))
    gv = gup.view(np.uint16).reshape(256, 11264, 2)
    gup_lo = np.ascontiguousarray(gv[:, :, 0])
    gup_hi = np.ascontiguousarray(gv[:, :, 1])
    dn = np.ascontiguousarray(np.transpose(inp["L0_ffn_down_packed"]))
    dv = dn.view(np.uint16).reshape(704, 2048, 2)
    dn_lo = np.ascontiguousarray(dv[:, :, 0])
    dn_hi = np.ascontiguousarray(dv[:, :, 1])
    gsc = inp["L0_ffn_gate_up_scales"]
    gup_scale = np.ascontiguousarray(
        np.stack([np.repeat(gsc[8 * w:8 * w + 8], 16, axis=0) for w in range(2)]))
    dsc = np.concatenate([inp["L0_ffn_down_scales"],
                          np.zeros((4, 2048), inp["L0_ffn_down_scales"].dtype)], axis=0)
    dn_scale = np.ascontiguousarray(
        np.stack([np.repeat(dsc[8 * w:8 * w + 8], 16, axis=0) for w in range(6)]))

    selmat = np.zeros((32, ND * 128), dtype=BF)
    for k in range(ND):
        for p in range(128):
            selmat[(128 * k + p) // 64, 128 * k + p] = 1.0

    gatewT = np.transpose(inp["L1_ffn_gate_w"])
    gatew_b = np.ascontiguousarray(gatewT.reshape(16, 128, 16).transpose(1, 0, 2)
                                   .reshape(128, 256))
    onehots = np.zeros((16, 16 * 128), dtype=BF)
    for e in range(16):
        onehots[e, 128 * e:128 * (e + 1)] = 1.0
    w13_t = np.stack([_tile_emajor(np.ascontiguousarray(np.transpose(inp["L1_ffn_w13"][e])),
                                   D, 2 * I_MOE) for e in range(E_MOE)])
    w2_t = np.stack([_tile_emajor(np.ascontiguousarray(np.transpose(inp["L1_ffn_w2"][e])),
                                  I_MOE, D) for e in range(E_MOE)])
    sw13_t = _tile_emajor(np.ascontiguousarray(np.transpose(inp["L1_ffn_sw13"])),
                          D, 2 * IS_MOE)
    sw2_t = _tile_emajor(np.ascontiguousarray(np.transpose(inp["L1_ffn_sw2"])),
                         IS_MOE, D)
    identf_in = np.eye(128, dtype=np.float32)

    wkv = np.concatenate([inp["L1_attn_Wk"], inp["L1_attn_Wv"]], axis=1)
    wkv_b = np.ascontiguousarray(wkv.reshape(16, 128, 1024).transpose(1, 0, 2)
                                 .reshape(128, 16 * 1024))
    inv_freq = 1.0 / (10000.0 ** (np.arange(0, HD1, 2, dtype=np.float64) / HD1))

    shared = dict(
        winT_t=winT_t, woutT=woutT, wdtT_r=wdtT_r,
        wkv_b=wkv_b, wq=np.ascontiguousarray(inp["L1_attn_Wq"]),
        wo=np.ascontiguousarray(inp["L1_attn_Wo"]),
        gatew_b=gatew_b, onehots=onehots, w13_t=w13_t, w2_t=w2_t,
        sw13_t=sw13_t, sw2_t=sw2_t, identf_in=identf_in,
        logA_t=logA_t, b_dt=b_dt, selmat=selmat,
        gup_lo=gup_lo, gup_hi=gup_hi, dn_lo=dn_lo, dn_hi=dn_hi,
        gup_scale=gup_scale, dn_scale=dn_scale,
    )
    in_maps = []
    for c in range(8):
        b, h = c // 2, c % 2
        m = dict(shared)
        m["x0_own"] = np.ascontiguousarray(x0[b, h * TPC:(h + 1) * TPC])
        m["x0_pre"] = (np.zeros((TPC, D), np.float32) if h == 0
                       else np.ascontiguousarray(x0[b, 0:TPC]))
        pos = np.arange(h * TPC, (h + 1) * TPC, dtype=np.float64)
        fr = pos[:, None] * inv_freq[None, :]
        emb = np.concatenate([fr, fr], axis=1)
        cosq = np.cos(emb).astype(np.float32)
        sinq = np.sin(emb).astype(np.float32)
        m["cosq"] = np.ascontiguousarray(cosq)
        m["sinq"] = np.ascontiguousarray(sinq)
        m["cosk"] = np.ascontiguousarray(np.tile(cosq, (1, 4)))
        m["sink"] = np.ascontiguousarray(np.tile(sinq, (1, 4)))
        in_maps.append(m)
    return in_maps


def alloc_cst(S):
    cstt = S("cstt", [128, 16], F32)
    return {"c1": cstt[:, 0:1], "cm1": cstt[:, 1:2], "ceps": cstt[:, 2:3],
            "cmhalf": cstt[:, 3:4], "m1": cstt[:, 4:5], "m2": cstt[:, 5:6],
            "m3": cstt[:, 6:7], "m4": cstt[:, 7:8], "m5": cstt[:, 8:9],
            "m6": cstt[:, 9:10], "m7": cstt[:, 10:11],
            "cqscale": cstt[:, 11:12], "cqbias": cstt[:, 12:13], "_t": cstt}


def init_cst(em, cstt):
    s_init = em.sem("init")
    for col, v in ((0, 1.0), (1, -1.0), (2, 1e-5), (3, -0.5),
                   (11, 1.0 / math.sqrt(128.0)), (12, -12.0)):
        em.op("vector", "memset", sem=s_init, ap=cstt[:, col:col + 1], constant=v)
    for e in ENGINES:
        em.wait(e, s_init)


def alloc_l0a(S, sb):
    sb["arenaA"] = S("arenaA", [128, ND * 2048], BF16)
    sb["arenaB"] = S("arenaB", [128, ND * 2048], BF16)
    sb["rhalf"] = sb["arenaB"].bitcast(F32)
    sb["a2"] = [S("a2_0", [128, 2048], F32), S("a2_1", [128, 2048], F32)]
    sb["prescan"] = S("prescan", [128, 512], F32)
    sb["dt1"] = S("dt1", [32, 2048], F32)
    sb["dt2"] = S("dt2", [32, 2048], F32)
    sb["dtbf"] = S("dtbf", [32, 2048], BF16)
    sb["wdt"] = S("wdt", [128, 512], BF16)
    sb["selmat"] = S("selmat_sb", [32, ND * 128], BF16)
    sb["acraw"] = S("acraw", [128, 16], F32)
    sb["acont"] = S("acont", [128, 16], F32)
    sb["bdt"] = S("bdt", [32, 1], F32)
    sb["states"] = S("states", [128, 16], F32)
    sb["wincol"] = [S("wincol_0", [128, ND * 128], BF16),
                    S("wincol_1", [128, ND * 128], BF16)]
    sb["wstage"] = [S("wstage0", [128, 2048], BF16), S("wstage1", [128, 2048], BF16)]
    sb["xstage"] = [S("xstage0", [128, 2048], BF16), S("xstage1", [128, 2048], BF16)]
    sb["wslice"] = [S("wslice_%d" % i, [128, 512], BF16) for i in range(3)]
    sb["x0t"] = [S("x0t_0", [128, 512], BF16), S("x0t_1", [128, 512], BF16)]
    sb["psums"] = S("psums", [128, 32], F32)
    sb["psq"] = S("psq", [128, 4], F32)
    sb["sqscr"] = S("sqscr", [128, 512], BF16)
    sb["oxt"] = S("oxt", [128, 2048], BF16)


def alloc_dq(S, sb):
    CH = 2816
    sb["dq_src"] = [S("dq_src0", [128, CH], U16), S("dq_src1", [128, CH], U16)]
    sb["dq_scl"] = [S("dq_scl0", [128, CH], F16), S("dq_scl1", [128, CH], F16)]
    sb["dq_m3"] = S("dq_m3", [128, CH], U16)
    sb["dq_ta"] = S("dq_ta", [128, CH], U16)
    sb["dq_fb"] = S("dq_fb", [128, CH], U16)
    sb["dq_sg"] = S("dq_sg", [128, CH], U16)
    sb["dq_w"] = [S("dq_w0", [128, CH], BF16), S("dq_w1", [128, CH], BF16)]


def _mk_nc():
    nc = bass.Bass("TRN2", target_bir_lowering=False, debug=False)
    return nc


def build_phase(phase, io_ext):
    nc = _mk_nc()
    ctx = contextlib.ExitStack()
    io = {}
    for name, (shape, dtype, kind) in io_ext.items():
        io[name] = nc.dram_tensor(name, list(shape), dtype, kind=kind).ap()
    with ctx:
        sb = {}

        def S(name, shape, dtype=F32):
            t = ctx.enter_context(nc.sbuf_tensor(name, shape, dtype))
            return t.ap()

        em = Em(nc, ctx)
        if phase.startswith("l0a"):
            sb["cst"] = alloc_cst(S)
            alloc_l0a(S, sb)
            psA = ctx.enter_context(nc.psum_tensor("psA", [128, 2048], F32)).ap()
            psB = ctx.enter_context(nc.psum_tensor("psB", [128, 2048], F32)).ap()
            init_cst(em, sb["cst"]["_t"])
            stop = phase[4:] if len(phase) > 3 and "_" in phase else None
            emit_l0a(em, nc, sb, io, psA, psB, stop_after=stop)
        elif phase == "dq":
            alloc_dq(S, sb)
            emit_dq(em, nc, sb, io)
        elif phase.startswith("moe"):
            sb["cst"] = alloc_cst(S)
            alloc_moe(S, sb)
            psA = ctx.enter_context(nc.psum_tensor("psA", [128, 2048], F32)).ap()
            psB = ctx.enter_context(nc.psum_tensor("psB", [128, 2048], F32)).ap()
            init_cst(em, sb["cst"]["_t"])
            ne = E_MOE if "_" not in phase else int(phase.split("_")[1])
            emit_moe(em, nc, sb, io, psA, psB, n_experts=ne)
        elif phase.startswith("l1a"):
            sb["cst"] = alloc_cst(S)
            alloc_l1a(S, sb)
            psA = ctx.enter_context(nc.psum_tensor("psA", [128, 2048], F32)).ap()
            psB = ctx.enter_context(nc.psum_tensor("psB", [128, 2048], F32)).ap()
            init_cst(em, sb["cst"]["_t"])
            em.op("vector", "memset", ap=sb["ones"], constant=1.0)
            stop = phase[4:] if "_" in phase else None
            emit_l1a(em, nc, sb, io, psA, psB, stop_after=stop)
        elif phase.startswith("l0f"):
            sb["cst"] = alloc_cst(S)
            alloc_l0f(S, sb)
            psA = ctx.enter_context(nc.psum_tensor("psA", [128, 2048], F32)).ap()
            psB = ctx.enter_context(nc.psum_tensor("psB", [128, 2048], F32)).ap()
            init_cst(em, sb["cst"]["_t"])
            stop = phase[4:] if "_" in phase else None
            emit_l0f(em, nc, sb, io, psA, psB, stop_after=stop)
        else:
            raise ValueError(phase)
        with nc.Block() as block:
            em.finish(block)
    return nc


IO_L0A = dict(
    x0_own=((TPC, D), F32, "ExternalInput"),
    x0_pre=((TPC, D), F32, "ExternalInput"),
    winT_t=((ND, 128, D), F32, "ExternalInput"),
    woutT=((D, D), F32, "ExternalInput"),
    wdtT_r=((128, 512), F32, "ExternalInput"),
    logA_t=((128, 16), F32, "ExternalInput"),
    b_dt=((32, 1), F32, "ExternalInput"),
    selmat=((32, ND * 128), BF16, "ExternalInput"),
    x1s=((TPC, D), BF16, "ExternalOutput"),
    xbf_own=((TPC, D), BF16, "Internal"),
    xbf_pre=((TPC, D), BF16, "Internal"),
    woutT_s=((ND, 128, D), BF16, "Internal"),
)

IO_DQ = dict(
    gup_lo=((256, 11264), U16, "ExternalInput"),
    gup_hi=((256, 11264), U16, "ExternalInput"),
    dn_lo=((704, D), U16, "ExternalInput"),
    dn_hi=((704, D), U16, "ExternalInput"),
    gup_scale=((2, 128, 11264), F16, "ExternalInput"),
    dn_scale=((6, 128, D), F16, "ExternalInput"),
    wgu_t=((88, 128, 16, 128), BF16, "ExternalOutput"),
    wdn=((I_GU, D), BF16, "ExternalOutput"),
)


IO_FUSED_EXT = {}


def build_fused(upto=5):
    nc = _mk_nc()
    ctx = contextlib.ExitStack()
    io = {}

    def dram(name, spec, kind):
        io[name] = nc.dram_tensor(name, list(spec[0]), spec[1], kind=kind).ap()

    internal = {"x1s", "x2s", "x3s", "wgu_t", "wdn", "xbf_own", "xbf_pre",
                "woutT_s", "cck_in", "ccv_in", "cck_out", "ccv_out", "qscr"}
    last_stream = {1: "wgu_t", 2: "x1s", 3: "x2s", 4: "x3s", 5: None}[upto]
    if last_stream:
        internal = internal - {last_stream}
    iods = (IO_DQ, IO_L0A, IO_L0F, IO_L1A, IO_MOE)[:upto]
    seen = set()
    for iod in iods:
        for name, spec in iod.items():
            if name in seen:
                continue
            seen.add(name)
            if name == "outs" or name == last_stream:
                dram(name, spec, "ExternalOutput")
            elif name in internal:
                dram(name, spec, "Internal")
            else:
                dram(name, spec, "ExternalInput")

    with ctx:
        psA = ctx.enter_context(nc.psum_tensor("psA", [128, 2048], F32)).ap()
        psB = ctx.enter_context(nc.psum_tensor("psB", [128, 2048], F32)).ap()
        em = Em(nc, ctx)
        cstt_t = ctx.enter_context(nc.sbuf_tensor("cstt", [128, 16], F32))
        cstt = cstt_t.ap()
        cst = {"c1": cstt[:, 0:1], "cm1": cstt[:, 1:2], "ceps": cstt[:, 2:3],
               "cmhalf": cstt[:, 3:4], "m1": cstt[:, 4:5], "m2": cstt[:, 5:6],
               "m3": cstt[:, 6:7], "m4": cstt[:, 7:8], "m5": cstt[:, 8:9],
               "m6": cstt[:, 9:10], "m7": cstt[:, 10:11],
               "cqscale": cstt[:, 11:12], "cqbias": cstt[:, 12:13], "_t": cstt}
        init_cst(em, cstt)

        def phase(allocf, emitf, prefix, extra=None):
            with contextlib.ExitStack() as pctx:
                sb = {"cst": cst}

                def S(name, shape, dtype=F32):
                    t = pctx.enter_context(
                        nc.sbuf_tensor(prefix + name, shape, dtype))
                    return t.ap()
                if allocf is not None:
                    allocf(S, sb)
                if extra is not None:
                    extra(em, sb)
                emitf(em, nc, sb, io, psA, psB)
            em.new_phase()

        phase(alloc_dq, lambda em, nc, sb, io, a, b: emit_dq(em, nc, sb, io), "d_")
        if upto >= 2:
            phase(alloc_l0a, emit_l0a, "a_")
        if upto >= 3:
            phase(alloc_l0f, emit_l0f, "f_")
        if upto >= 4:
            def l1a_extra(em, sb):
                em.op("vector", "memset", ap=sb["ones"], constant=1.0)
            phase(alloc_l1a, emit_l1a, "t_", extra=l1a_extra)
        if upto >= 5:
            phase(alloc_moe, emit_moe, "m_")

        with nc.Block() as block:
            em.finish(block)
    return nc


_NC_CACHE = {}


def _get_nc(phase, io):
    if phase not in _NC_CACHE:
        _NC_CACHE[phase] = build_phase(phase, io)
    return _NC_CACHE[phase]


def kernel(**inputs):
    maps = prep_inputs(inputs)
    if "fused" not in _NC_CACHE:
        _NC_CACHE["fused"] = build_fused()
    nc = _NC_CACHE["fused"]
    names = [a.memorylocations[0].name
             for a in nc.m.functions[0].allocations
             if getattr(a, "kind", None) == "ExternalInput"
             and getattr(a, "memorylocations", None)]
    names = [n for n in names if n in maps[0]]
    in_maps = [{k: m[k] for k in names} for m in maps]
    res = run_bass_kernel_spmd(nc, in_maps, core_ids=list(range(8)))
    out = np.empty((B, T, D), np.float32)
    for c in range(8):
        b, h = c // 2, c % 2
        out[b, h * TPC:(h + 1) * TPC] = res.results[c]["outs"]
    return out


def kernel_unfused(**inputs):
    maps = prep_inputs(inputs)

    nc = _get_nc("dq", IO_DQ)
    dq_in = {k: maps[0][k] for k, v in IO_DQ.items() if v[2] == "ExternalInput"}
    res = run_bass_kernel_spmd(nc, [dq_in], core_ids=[0])
    wgu_t = res.results[0]["wgu_t"]
    wdn = res.results[0]["wdn"]

    nc = _get_nc("l0a", IO_L0A)
    names = [k for k, v in IO_L0A.items() if v[2] == "ExternalInput"]
    res = run_bass_kernel_spmd(nc, [{k: m[k] for k in names} for m in maps],
                               core_ids=list(range(8)))
    x1 = [res.results[c]["x1s"] for c in range(8)]

    nc = _get_nc("l0f", IO_L0F)
    res = run_bass_kernel_spmd(
        nc, [{"x1s": x1[c], "wgu_t": wgu_t, "wdn": wdn} for c in range(8)],
        core_ids=list(range(8)))
    x2 = [res.results[c]["x2s"] for c in range(8)]

    nc = _get_nc("l1a", IO_L1A)
    names = [k for k, v in IO_L1A.items() if v[2] == "ExternalInput"]
    in_maps = []
    for c in range(8):
        m = {k: maps[c][k] for k in names if k != "x2s"}
        m["x2s"] = x2[c]
        in_maps.append(m)
    res = run_bass_kernel_spmd(nc, in_maps, core_ids=list(range(8)))
    x3 = [res.results[c]["x3s"] for c in range(8)]

    nc = _get_nc("moe", IO_MOE)
    names = [k for k, v in IO_MOE.items() if v[2] == "ExternalInput"]
    in_maps = []
    for c in range(8):
        m = {k: maps[c][k] for k in names if k != "x3s"}
        m["x3s"] = x3[c]
        in_maps.append(m)
    res = run_bass_kernel_spmd(nc, in_maps, core_ids=list(range(8)))

    out = np.empty((B, T, D), np.float32)
    for c in range(8):
        b, h = c // 2, c % 2
        out[b, h * TPC:(h + 1) * TPC] = res.results[c]["outs"]
    return out
